# version 32
# baseline (speedup 1.0000x reference)
import numpy as np
import ml_dtypes
from contextlib import ExitStack

import concourse.bass as bass
import concourse.mybir as mybir
from concourse.bass_utils import run_bass_kernel_spmd

F32 = mybir.dt.float32
BF16 = mybir.dt.bfloat16
I32 = mybir.dt.int32
AF = mybir.ActivationFunctionType
ALU = mybir.AluOpType

D = 1024
DT = 8
H = 16
HD = 64
TT = 512
CH = 128
NCH = TT // CH
FT = 256
HFF = 3072
HT = HFF // 128
BASEF = 10000.0
SC = 1.0 / 127.0
SC2 = SC * SC

ENGS = ("T", "V", "A", "G", "S")


class Sched:

    DMA_POOL = 8

    def __init__(self):
        self.stream = {e: [] for e in ENGS}
        self.counts = {}
        self.lastw = {}
        self.readers = {}
        self.wm = {e: {} for e in ENGS}
        self.pend = {e: [] for e in ENGS}
        self.dmacnt = {e: 0 for e in ENGS}

    def op(self, eng, fn, reads=(), writes=(), dma=False):
        deps = []
        for k in reads:
            t = self.lastw.get(k)
            if t is not None:
                deps.append(t)
        for k in writes:
            t = self.lastw.get(k)
            if t is not None:
                deps.append(t)
            deps.extend(self.readers.get(k, ()))
        deps.extend(self.pend[eng])
        self.pend[eng] = []
        if dma:
            i = self.dmacnt[eng]
            self.dmacnt[eng] = i + 1
            sem = f"d{eng}{i % self.DMA_POOL}"
            cyc = i // self.DMA_POOL
            if cyc > 0:
                deps.append((sem, 16 * cyc))
            amt = 16
            tokv = 16 * (cyc + 1)
        else:
            sem = "s" + eng
            amt = 1
            tokv = self.counts.get(sem, 0) + 1
        byse = {}
        for (s, v) in deps:
            if s == "sT" and eng == "T":
                continue
            if self.wm[eng].get(s, 0) >= v:
                continue
            byse[s] = max(byse.get(s, 0), v)
        waits = sorted(byse.items())
        for (s, v) in waits:
            self.wm[eng][s] = v
        self.counts[sem] = tokv
        tok = (sem, tokv)
        self.stream[eng].append((waits, fn, sem, amt))
        for k in reads:
            self.readers.setdefault(k, []).append(tok)
        for k in writes:
            self.lastw[k] = tok
            self.readers[k] = []
        return tok

    def barrier(self):
        toks = list(self.counts.items())
        for e in ENGS:
            self.pend[e] = list(toks)


def build_program(nown, npre):
    NT = nown // TT
    NPT = npre // TT
    NBT = nown // FT

    nc = bass.Bass("TRN2", target_bir_lowering=False, debug=False)

    def din(name, shape, dt=F32):
        return nc.dram_tensor(name, list(shape), dt, kind="ExternalInput").ap()

    xo = din("xo", [D, nown])
    xp = din("xp", [D, npre])
    cto = din("cto", [D, nown], BF16)
    sto = din("sto", [D, nown], BF16)
    ctp = din("ctp", [npre, D], BF16)
    stp = din("stp", [npre, D], BF16)
    wq_h = din("wq", [D, D])
    wk_h = din("wk", [D, D])
    wv_h = din("wv", [D, D])
    wo_h = din("wo", [D, D])
    wg1_h = din("wg1", [D, HD])
    wg2_h = din("wg2", [HD, D])
    normc_h = din("normc", [128, DT], F32)
    maskb_h = din("maskb", [128, 1], F32)
    ident_h = din("ident", [128, 128], BF16)
    onescol_h = din("onescol", [128, 1], BF16)
    onesrow_h = din("onesrow", [1, 128], BF16)
    mask4_h = din("mask4", [128, 4 * CH], BF16)
    blk2_h = din("blk2", [128, 2], BF16)
    blk2t_h = din("blk2t", [2, 128], BF16)
    onescan_h = din("onescan", [128, TT], BF16)
    onesone_h = din("onesone", [1, 1], BF16)
    wgt_h = din("wgt", [HT * D, 128], I32)
    wut_h = din("wut", [HT * D, 128], I32)
    wdt_h = din("wdt", [HFF, D], I32)

    x1h = nc.dram_tensor("x1h", [D, nown], BF16, kind="Internal").ap()
    outT = nc.dram_tensor("outT", [D, nown], F32, kind="ExternalOutput").ap()

    sched = Sched()
    S = sched.op

    with ExitStack() as ctx:
        NBF = 96688
        NF = 4632
        arena = ctx.enter_context(nc.sbuf_tensor("arena", [128, NBF], BF16))
        arenaf = ctx.enter_context(nc.sbuf_tensor("arenaf", [128, NF], F32))

        cur = [0]

        def abf(nelem, rows=128):
            o = cur[0]
            assert o + nelem * 2 <= NBF * 2, "bf arena overflow"
            cur[0] += ((nelem * 2 + 31) // 32) * 32
            return arena[0:rows, o // 2: o // 2 + nelem]

        curf = [0]

        def af32(nelem, rows=128):
            o = curf[0]
            assert o + nelem * 4 <= NF * 4, "f32 arena overflow"
            curf[0] += ((nelem * 4 + 31) // 32) * 32
            return arenaf[0:rows, o // 4: o // 4 + nelem]

        ident = abf(128)
        onescol = abf(1)
        onesrow = abf(128, rows=1)
        mask4 = abf(4 * CH)
        blk2 = abf(2)
        blk2t = abf(128, rows=2)
        onescan = abf(TT)
        onesone = abf(1, rows=1)
        maskb = af32(1)
        normc = af32(DT)
        nsb = abf(H * HD)
        abrow = abf(nown, rows=1)
        bbrow = abf(nown, rows=1)
        zrun = af32(DT)
        pers_end = (cur[0], curf[0])

        xb = [abf(DT * TT), abf(DT * TT)]
        ct = abf(DT * TT)
        st = abf(DT * TT)
        xln = abf(DT * TT)
        eo = abf(DT * TT)
        big2 = abf(2 * DT * TT)
        qh = abf(H * TT)
        vb = [abf(D), abf(D)]
        etok = abf(2 * D)
        scbs = [abf(4 * CH), abf(4 * CH)]
        qpres = [abf(TT), abf(TT)]
        sqs = [abf(TT), abf(TT)]
        g1sb = abf(TT, rows=64)
        rba = abf(TT, rows=1)
        rbb = abf(TT, rows=1)
        r2bfs = [abf(TT, rows=2), abf(TT, rows=2)]
        bca = abf(TT)
        bcb = abf(TT)
        wqs = abf(DT * D)
        wks = abf(DT * D)
        wvs = abf(DT * D)
        wos = abf(DT * D)
        wg1s = abf(DT * HD)
        wg2s = abf(D, rows=64)
        t1fs = [af32(TT), af32(TT)]
        zf = [af32(TT), af32(TT)]
        rs = [af32(TT, rows=2) for _ in range(4)]
        a_end = (cur[0], curf[0])

        cur[0], curf[0] = pers_end
        wgall = abf(HT * D)
        wuall = abf(HT * D)
        wdall = abf(HT * D)
        x1t = [abf(DT * FT), abf(DT * FT)]
        hb = [abf(FT), abf(FT)]
        bcab = abf(2 * FT)
        x1fs = [af32(DT * FT), af32(DT * FT)]
        silubs = [af32(FT), af32(FT)]
        b_end = (cur[0], curf[0])
        assert max(a_end[0], b_end[0]) <= NBF * 2, (a_end, b_end)
        assert max(a_end[1], b_end[1]) <= NF * 4, (a_end, b_end)

        sebf = qh[0:1, 0:D]
        xv = [xb[i].rearrange("p (c t) -> p c t", c=DT) for i in range(2)]
        ctv = ct.rearrange("p (c t) -> p c t", c=DT)
        stv = st.rearrange("p (c t) -> p c t", c=DT)
        ctok = ct.rearrange("p (c t) -> p c t", c=NCH)[:, :, 0:D]
        stok = st.rearrange("p (c t) -> p c t", c=NCH)[:, :, 0:D]
        eov = eo.rearrange("p (c t) -> p c t", c=DT)
        bigv = big2.rearrange("p (c t) -> p c t", c=2 * DT)

        pbk = [ctx.enter_context(nc.psum_tensor(f"pb{i}", [128, 512], F32))
               for i in range(8)]
        pb = pbk
        pbb = [pbk[i].bitcast(BF16) for i in range(8)]

        sems = {}
        semnames = ["sT", "sV", "sA", "sG"]
        for q in ("S", "G", "A"):
            for r in range(Sched.DMA_POOL):
                semnames.append(f"d{q}{r}")
        for s in semnames:
            sems[s] = ctx.enter_context(nc.semaphore(s))

        for (dst, src) in ((ident, ident_h), (onescol, onescol_h),
                           (onesrow, onesrow_h), (mask4, mask4_h),
                           (blk2, blk2_h), (blk2t, blk2t_h),
                           (onescan, onescan_h), (onesone, onesone_h),
                           (maskb, maskb_h), (normc, normc_h)):
            S("S", lambda e, d=dst, s=src: e.dma_start(out=d, in_=s[:, :]),
              writes=("kconst",), dma=True)

        def wload(dst, src, key):
            r = src.rearrange("(c p) o -> p c o", p=128)
            nco = r.shape[1]
            d3 = dst.rearrange("p (c o) -> p c o", c=nco)
            S("G", lambda e, d=d3, s=r: e.dma_start(out=d, in_=s),
              writes=(key,), dma=True)

        wload(wks, wk_h, "wk")
        wload(wvs, wv_h, "wv")

        def loadx(src, j, buf):
            s3 = src.rearrange("(c p) t -> p c t", p=128)[:, :, j * TT:(j + 1) * TT]
            S("G", lambda e, d=xv[buf], s=s3: e.dma_start(out=d, in_=s),
              writes=(f"x{buf}",), dma=True)

        def loadtab_feat(j):
            c3 = cto.rearrange("(c p) t -> p c t", p=128)[:, :, j * TT:(j + 1) * TT]
            s3 = sto.rearrange("(c p) t -> p c t", p=128)[:, :, j * TT:(j + 1) * TT]
            S("S", lambda e, d=ctv, s=c3: e.dma_start(out=d, in_=s), writes=("ct",), dma=True)
            S("S", lambda e, d=stv, s=s3: e.dma_start(out=d, in_=s), writes=("st",), dma=True)

        def loadtab_tok(j):
            c3 = ctp[j * TT:(j + 1) * TT, :].rearrange("(c p) e -> p c e", p=128)
            s3 = stp[j * TT:(j + 1) * TT, :].rearrange("(c p) e -> p c e", p=128)
            S("S", lambda e, d=ctok, s=c3: e.dma_start(out=d, in_=s), writes=("ct",), dma=True)
            S("S", lambda e, d=stok, s=s3: e.dma_start(out=d, in_=s), writes=("st",), dma=True)

        def MM(bank, out, lhsT, rhs, start, stop, reads):
            S("T", lambda e, o=out, l=lhsT, r=rhs, s=start, p=stop:
              e.matmul(o, l, r, start=s, stop=p), reads=reads, writes=(bank,))

        def ACOPY(out, in_, reads, writes):
            S("A", lambda e, o=out, i=in_: e.copy(o, i), reads=reads,
              writes=writes)

        def ln_step(xkey, xvw, b0, b1, dd, sqbuf, sqkey, eng="V"):
            S(eng, lambda e, o=sqbuf, a=xvw[:, dd, :]:
              e.tensor_tensor(out=o, in0=a, in1=a, op=ALU.mult),
              reads=(xkey,), writes=(sqkey,))
            MM(f"p{b0}", pb[b0][0:1, 0:TT], onescol, xvw[:, dd, :],
               dd == 0, dd == DT - 1, (xkey, "kconst"))
            MM(f"p{b1}", pb[b1][0:1, 0:TT], onescol, sqbuf,
               dd == 0, dd == DT - 1, (sqkey, "kconst"))

        def layernorm(xkey, xvw, b0, b1, stats_done=False):
            if not stats_done:
                for dd in range(DT):
                    ln_step(xkey, xvw, b0, b1, dd, sqs[dd % 2], f"sq{dd % 2}")
            r0, r1, r2, r3 = rs
            S("V", lambda e: e.tensor_scalar(out=r0[0:1, :], in0=pb[b0][0:1, 0:TT],
              scalar1=1.0 / D, scalar2=None, op0=ALU.mult),
              reads=(f"p{b0}",), writes=("r0",))
            S("V", lambda e: e.tensor_scalar(out=r1[0:1, :], in0=pb[b1][0:1, 0:TT],
              scalar1=1.0 / D, scalar2=None, op0=ALU.mult),
              reads=(f"p{b1}",), writes=("r1",))
            S("V", lambda e: e.tensor_tensor(out=r2[0:1, :], in0=r0[0:1, :],
              in1=r0[0:1, :], op=ALU.mult), reads=("r0",), writes=("r2",))
            S("V", lambda e: e.scalar_tensor_tensor(out=r2[0:1, :], in0=r1[0:1, :],
              scalar=1e-5, in1=r2[0:1, :], op0=ALU.add, op1=ALU.subtract),
              reads=("r1", "r2"), writes=("r2",))
            S("A", lambda e: e.activation(out=r3[0:1, :], in_=r2[0:1, :],
              func=AF.Ln), reads=("r2",), writes=("r3",))
            S("A", lambda e: e.activation(out=rba[0:1, :], in_=r3[0:1, :],
              func=AF.Exp, scale=-0.5), reads=("r3",), writes=("rba",))
            S("V", lambda e: e.scalar_tensor_tensor(out=rbb[0:1, :], in0=r0[0:1, :],
              scalar=-1.0, in1=rba[0:1, :], op0=ALU.mult, op1=ALU.mult),
              reads=("r0", "rba"), writes=("rbb",))
            MM(f"p{b0}", pb[b0][:, 0:TT], onesrow, rba[0:1, :], True, True,
               ("rba", "kconst"))
            MM(f"p{b1}", pb[b1][:, 0:TT], onesrow, rbb[0:1, :], True, True,
               ("rbb", "kconst"))
            S("A", lambda e: e.copy(bca, pb[b0][:, 0:TT]),
              reads=(f"p{b0}",), writes=("bca",))
            S("A", lambda e: e.copy(bcb, pb[b1][:, 0:TT]),
              reads=(f"p{b1}",), writes=("bcb",))
            xlv = xln.rearrange("p (c t) -> p c t", c=DT)
            for dd in range(DT):
                S("V", lambda e, d=xlv[:, dd, :], a=xvw[:, dd, :]:
                  e.tensor_tensor(out=d, in0=a, in1=bca, op=ALU.mult),
                  reads=(xkey, "bca"), writes=(f"xl{dd}",))
                S("V", lambda e, d=xlv[:, dd, :]:
                  e.tensor_tensor(out=d, in0=d, in1=bcb, op=ALU.add),
                  reads=("bcb",), writes=(f"xl{dd}",))
            return xlv

        first_n = [True]
        for jp in range(NPT):
            buf = jp % 2
            loadx(xp, jp, buf)
            loadtab_tok(jp)
            if jp == 1 or (NPT == 1 and jp == 0):
                wload(wqs, wq_h, "wq")
                wload(wos, wo_h, "wo")
                wload(wg1s, wg1_h, "wg1")
                S("G", lambda e: e.dma_start(out=wg2s, in_=wg2_h[:, :]),
                  writes=("wg2",), dma=True)
                for dd in range(DT):
                    S("V", lambda e, dd=dd: e.tensor_scalar(
                        out=wos[:, dd * D:(dd + 1) * D],
                        in0=wos[:, dd * D:(dd + 1) * D],
                        scalar1=normc[:, dd:dd + 1], scalar2=None, op0=ALU.mult),
                        reads=("kconst",), writes=("wo",))
            xlv = layernorm(f"x{buf}", xv[buf], 2, 3)

            def pre_tail(cc, jp=jp):
                gfirst = (jp == 0 and cc == 0)
                glast_pre = (jp == NPT - 1 and cc == NCH - 1)
                MM("p6", pb[6][0:1, 0:512], onescol, eo[:, 0:512],
                   gfirst, glast_pre, ("eo0", "kconst"))
                MM("p7", pb[7][0:1, 0:512], onescol, eo[:, 512:1024],
                   gfirst, glast_pre, ("eo1", "kconst"))
                for h in range(H):
                    bank = h // 8
                    MM(f"p{bank}", pb[bank][:, (h % 8) * HD:(h % 8) * HD + HD],
                       etok[:, h * 128:(h + 1) * 128],
                       vb[cc % 2][:, h * HD:(h + 1) * HD],
                       first_n[0] and h % 8 == 0, h % 8 == 7,
                       (f"etok{h // 4}", f"vb{cc % 2}"))
                first_n[0] = False

            for cc in range(NCH):
                vbuf = cc % 2
                for half in range(2):
                    bank = 2 + half
                    for dd in range(DT):
                        MM(f"p{bank}", pb[bank][:, 0:512],
                           xlv[:, dd, cc * CH:(cc + 1) * CH],
                           wks[:, dd * D + half * 512: dd * D + half * 512 + 512],
                           dd == 0, dd == DT - 1, (f"xl{dd}", "wk"))
                for half in range(2):
                    bank = 4 + half
                    for dd in range(DT):
                        MM(f"p{bank}", pb[bank][:, 0:512],
                           xlv[:, dd, cc * CH:(cc + 1) * CH],
                           wvs[:, dd * D + half * 512: dd * D + half * 512 + 512],
                           dd == 0, dd == DT - 1, (f"xl{dd}", "wv"))
                if cc > 0:
                    pre_tail(cc - 1)
                for half in range(2):
                    bank = 2 + half
                    S("A", lambda e, b=bank, h=half: e.activation(
                        out=eo[:, h * 512:(h + 1) * 512], in_=pb[b][:, 0:512],
                        func=AF.Exp), reads=(f"p{bank}",), writes=(f"eo{half}",))
                S("V", lambda e: e.tensor_scalar(out=eo[:, 0:D], in0=eo[:, 0:D],
                  scalar1=maskb, scalar2=None, op0=ALU.mult),
                  reads=("kconst",), writes=("eo0", "eo1"))
                ev = eo[:, 0:D].rearrange("p (h e) -> p h e", h=H)
                tv = etok.rearrange("p (h e) -> p h e", h=H)
                S("V", lambda e, cc=cc: e.tensor_tensor(
                    out=tv[:, :, 0:HD], in0=ev,
                    in1=ctok[:, cc, :].rearrange("p (h e) -> p h e", h=H),
                    op=ALU.mult), reads=("eo0", "eo1", "ct"),
                    writes=("etok0", "etok1", "etok2", "etok3"))
                S("V", lambda e, cc=cc: e.tensor_tensor(
                    out=tv[:, :, HD:128], in0=ev,
                    in1=stok[:, cc, :].rearrange("p (h e) -> p h e", h=H),
                    op=ALU.mult), reads=("eo0", "eo1", "st"),
                    writes=("etok0", "etok1", "etok2", "etok3"))
                for half in range(2):
                    bank = 4 + half
                    S("A", lambda e, b=bank, h=half, w=vbuf: e.copy(
                        vb[w][:, h * 512:(h + 1) * 512], pb[b][:, 0:512]),
                        reads=(f"p{bank}",), writes=(f"vb{vbuf}",))
            pre_tail(NCH - 1)

        if NPT > 0:
            S("A", lambda e: e.copy(sebf[0:1, 0:512], pb[6][0:1, 0:512]),
              reads=("p6",), writes=("qh0",))
            S("A", lambda e: e.copy(sebf[0:1, 512:1024], pb[7][0:1, 0:512]),
              reads=("p7",), writes=("qh1",))
            for jj in range(DT):
                MM("p2", pb[2][:, jj:jj + 1], sebf[0:1, jj * 128:(jj + 1) * 128],
                   onesone, jj == 0, jj == DT - 1, ("qh0", "qh1", "kconst"))
            S("V", lambda e: e.tensor_scalar(out=zrun[:, 0:DT], in0=pb[2][:, 0:DT],
              scalar1=1.0, scalar2=None, op0=ALU.add), reads=("p2",),
              writes=tuple(f"zrun{q}" for q in range(DT)))
            S("A", lambda e: e.copy(nsb[:, 0:512], pb[0][:, 0:512]),
              reads=("p0",), writes=("nsb",))
            S("A", lambda e: e.copy(nsb[:, 512:1024], pb[1][:, 0:512]),
              reads=("p1",), writes=("nsb",))
        else:
            S("V", lambda e: e.memset(zrun[:, 0:DT], 1.0),
              writes=tuple(f"zrun{q}" for q in range(DT)))
            S("V", lambda e: e.memset(nsb, 0.0), writes=("nsb",))

        loadx(xo, 0, 0)
        loadtab_feat(0)
        xlv = layernorm("x0", xv[0], 6, 7)
        for j in range(NT):
            buf = j % 2

            def s1_tail(jj):
                bq = 4 + (2 * jj + 1) % 4
                t1 = t1fs[jj % 2]
                tk = f"t1f{jj % 2}"
                qp = qpres[jj % 2]
                qk = f"qpre{jj % 2}"
                zb = zf[jj % 2]
                S("A", lambda e, z=zb, t=t1: e.activation(out=z, in_=t, func=AF.Ln),
                  reads=(tk,), writes=(f"zf{jj % 2}",))
                S("A", lambda e, z=zb: e.activation(out=z, in_=z, func=AF.Exp,
                  scale=-1.0), reads=(), writes=(f"zf{jj % 2}",))
                S("V", lambda e, b=bq, z=zb, q=qp: e.tensor_tensor(
                    out=q, in0=pb[b][:, 0:TT], in1=z, op=ALU.mult),
                    reads=(f"p{bq}", f"zf{jj % 2}"), writes=(qk,))
                for hh in range(2):
                    h = 2 * jj + hh
                    r0p, r1p = hh * 64, hh * 64 + 64
                    for (src, skey, dstv, dkey) in (
                            (qp, qk, qh, f"qh{h}"),
                            (eov[:, jj, :], f"eo{jj}", big2, f"bg{h}")):
                        base = h * TT
                        S("V", lambda e, s=src, d=dstv, b=base, r0=r0p, r1=r1p, jj=jj:
                          e.tensor_tensor(out=d[0:64, b:b + TT], in0=s[r0:r1, :],
                                          in1=ctv[r0:r1, jj, :], op=ALU.mult),
                          reads=(skey, "ct"), writes=(dkey,))
                        S("V", lambda e, s=src, d=dstv, b=base, r0=r0p, r1=r1p, jj=jj:
                          e.tensor_tensor(out=d[64:128, b:b + TT], in0=s[r0:r1, :],
                                          in1=stv[r0:r1, jj, :], op=ALU.mult),
                          reads=(skey, "st"), writes=(dkey,))

            for jj in range(DT):
                bank = 4 + (2 * jj) % 4
                bq = 4 + (2 * jj + 1) % 4
                t1 = t1fs[jj % 2]
                tk = f"t1f{jj % 2}"
                zb = zf[jj % 2]
                for dd in range(DT):
                    MM(f"p{bank}", pb[bank][:, 0:TT],
                       wks[:, dd * D + jj * 128: dd * D + jj * 128 + 128],
                       xlv[:, dd, :], dd == 0, dd == DT - 1, (f"xl{dd}", "wk"))
                S("A", lambda e, b=bank, jj=jj: e.activation(
                    out=eov[:, jj, :], in_=pb[b][:, 0:TT], func=AF.Exp),
                    reads=(f"p{bank}",), writes=(f"eo{jj}",))
                S("V", lambda e, z=zb, jj=jj: e.tensor_tensor_scan(
                    out=z, data0=onescan, data1=eov[:, jj, :],
                    initial=zrun[:, jj:jj + 1], op0=ALU.mult, op1=ALU.add),
                    reads=(f"eo{jj}", f"zrun{jj}", "kconst"), writes=(f"zf{jj % 2}",))
                S("V", lambda e, z=zb, jj=jj: e.tensor_copy(
                    out=zrun[:, jj:jj + 1], in_=z[:, TT - 1:TT]),
                    reads=(f"zf{jj % 2}",), writes=(f"zrun{jj}",))
                for dd in range(DT):
                    MM(f"p{bq}", pb[bq][:, 0:TT],
                       wqs[:, dd * D + jj * 128: dd * D + jj * 128 + 128],
                       xlv[:, dd, :], dd == 0, dd == DT - 1, (f"xl{dd}", "wq"))
                S("A", lambda e, b=bq, t=t1: e.activation(
                    out=t, in_=pb[b][:, 0:TT], func=AF.Exp, scale=-1.0),
                    reads=(f"p{bq}",), writes=(tk,))
                S("V", lambda e, z=zb, t=t1: e.scalar_tensor_tensor(
                    out=t, in0=t, scalar=1.0, in1=z,
                    op0=ALU.add, op1=ALU.mult),
                    reads=(f"zf{jj % 2}",), writes=(tk,))
                if jj > 0:
                    s1_tail(jj - 1)
            s1_tail(DT - 1)
            for dd in range(DT):
                MM("p2", pb[2][0:64, 0:TT], wg1s[:, dd * HD:(dd + 1) * HD],
                   xlv[:, dd, :], dd == 0, dd == DT - 1, (f"xl{dd}", "wg1"))
            S("V", lambda e: e.tensor_copy(out=g1sb, in_=pb[2][0:64, 0:TT]),
              reads=("p2",), writes=("g1sb",))

            GROUPS = [[0, 2, 4, 6], [8, 10, 12, 14], [1, 3, 5, 7], [9, 11, 13, 15]]
            POS = {}
            for g_, hs_ in enumerate(GROUPS):
                for i_, h_ in enumerate(hs_):
                    POS[h_] = g_ * 4 + i_
            def v_mms(cc):
                vbuf = cc % 2
                for half in range(2):
                    bank = 2 + half
                    for dd in range(DT):
                        MM(f"p{bank}", pb[bank][:, 0:512],
                           xlv[:, dd, cc * CH:(cc + 1) * CH],
                           wvs[:, dd * D + half * 512: dd * D + half * 512 + 512],
                           dd == 0, dd == DT - 1, (f"xl{dd}", "wv"))
                    S("A", lambda e, b=bank, h=half, w=vbuf: e.copy(
                        vb[w][:, h * 512:(h + 1) * 512], pb[b][:, 0:512]),
                        reads=(f"p{bank}",), writes=(f"vb{vbuf}",))

            for cc in range(NCH):
                g_first = (NPT == 0 and j == 0 and cc == 0)
                g_last = (j == NT - 1 and cc == NCH - 1)
                vbuf = cc % 2
                if cc == 0:
                    v_mms(0)

                def head_tail(g, cc=cc, vbuf=vbuf, g_first=g_first, g_last=g_last):
                    heads = GROUPS[g]
                    bo = 6 + g % 2
                    for i, h in enumerate(heads):
                        MM(f"p{bo}", pb[bo][0:64, i * 128:(i + 1) * 128],
                           nsb[:, h * HD:(h + 1) * HD],
                           qh[:, h * TT + cc * CH: h * TT + cc * CH + CH],
                           i == 0, False, ("nsb", f"qh{h}"))
                        MM(f"p{bo}", pb[bo][0:64, i * 128:(i + 1) * 128],
                           vb[vbuf][:, h * HD:(h + 1) * HD],
                           scbs[g % 2][:, i * 128: i * 128 + 128],
                           False, i == 3, (f"vb{vbuf}", f"scb{g % 2}"))
                    for i, h in enumerate(heads):
                        bank = h // 8
                        MM(f"p{bank}", pb[bank][:, (h % 8) * HD:(h % 8) * HD + HD],
                           etok[:, POS[h] * 128: POS[h] * 128 + 128],
                           vb[vbuf][:, h * HD:(h + 1) * HD],
                           g_first and g < 2 and i == 0,
                           g_last or (i == 3 and g >= 2),
                           (f"etok{POS[h] // 4}", f"vb{vbuf}"))
                    rr = 0 if g < 2 else 64
                    c0 = 0 if g % 2 == 0 else 4
                    src = pb[bo][0:64, 0:512].rearrange("p (c t) -> p c t", c=4)
                    dst = eov[rr:rr + 64, c0:c0 + 4, cc * CH:(cc + 1) * CH]
                    S("V", lambda e, s=src, d=dst: e.tensor_copy(out=d, in_=s),
                      reads=(f"p{bo}",),
                      writes=tuple(f"eo{c0 + q}" for q in range(4)))

                for g, heads in enumerate(GROUPS):
                    bsc = 2 + g % 2
                    btr = 4 + g % 2
                    for i, h in enumerate(heads):
                        S("T", lambda e, h=h, i=i, b=btr, cc=cc: e.transpose(
                            out=pbb[b][:, i * 128:(i + 1) * 128],
                            in_=big2[:, h * TT + cc * CH: h * TT + cc * CH + CH],
                            identity=ident),
                            reads=(f"bg{h}", "kconst"), writes=(f"p{btr}",))
                    S("A", lambda e, b=btr, g=g: e.copy(
                        etok[:, g * 512:(g + 1) * 512], pbb[b][:, 0:512]),
                        reads=(f"p{btr}",), writes=(f"etok{g}",))
                    for i, h in enumerate(heads):
                        MM(f"p{bsc}", pb[bsc][:, i * 128:(i + 1) * 128],
                           big2[:, h * TT + cc * CH: h * TT + cc * CH + CH],
                           qh[:, h * TT + cc * CH: h * TT + cc * CH + CH],
                           i == 0, i == 3, (f"bg{h}", f"qh{h}"))
                    S("V", lambda e, b=bsc, g=g: e.tensor_tensor(
                        out=scbs[g % 2][:, 0:512],
                        in0=pb[b][:, 0:512], in1=mask4, op=ALU.mult),
                        reads=(f"p{bsc}", "kconst"), writes=(f"scb{g % 2}",))
                    if g >= 1:
                        head_tail(g - 1)
                if cc + 1 < NCH:
                    v_mms(cc + 1)
                head_tail(3)
                S("V", lambda e: e.tensor_copy(out=nsb[:, 0:512], in_=pb[0][:, 0:512]),
                  reads=("p0",), writes=("nsb",))
                S("V", lambda e: e.tensor_copy(out=nsb[:, 512:1024], in_=pb[1][:, 0:512]),
                  reads=("p1",), writes=("nsb",))

            if j + 1 < NT:
                loadx(xo, j + 1, (j + 1) % 2)
                loadtab_feat(j + 1)

            def s3_tail(jj):
                bank = 2 + jj % 2
                t1 = t1fs[jj % 2]
                tk = f"t1f{jj % 2}"
                s1 = sqs[(jj + 1) % 2]
                sk = f"sq{(jj + 1) % 2}"
                p = jj % 2
                ra, ka = (rs[2], "r2") if p == 0 else (rs[0], "r0")
                rb, kb = (rs[3], "r3") if p == 0 else (rs[1], "r1")
                r2b = r2bfs[p]
                bred = 4 + p
                bbc = 5 - p
                S("A", lambda e, t=t1: e.activation(out=t, in_=t, func=AF.Ln),
                  reads=(tk,), writes=(tk,))
                S("A", lambda e, t=t1, o=s1: e.activation(out=o, in_=t, func=AF.Exp,
                  scale=-1.0), reads=(tk,), writes=(sk,))
                S("V", lambda e, jj=jj, o=s1: e.tensor_tensor(
                    out=eov[:, jj, :], in0=eov[:, jj, :], in1=o, op=ALU.mult),
                    reads=(sk,), writes=(f"eo{jj}",))
                sq = sqs[jj % 2]
                S("V", lambda e, jj=jj, o=sq: e.tensor_tensor(
                    out=o, in0=eov[:, jj, :], in1=eov[:, jj, :], op=ALU.mult),
                    reads=(f"eo{jj}",), writes=(f"sq{jj % 2}",))
                MM(f"p{bred}", pb[bred][0:2, 0:TT], blk2, sq, True, True,
                   (f"sq{jj % 2}", "kconst"))
                S("V", lambda e, b=bred, r=ra: e.tensor_scalar(out=r[0:2, :],
                  in0=pb[b][0:2, 0:TT], scalar1=1.0 / HD, scalar2=1e-6,
                  op0=ALU.mult, op1=ALU.add), reads=(f"p{bred}",), writes=(ka,))
                S("A", lambda e, r=ra, o=rb: e.activation(out=o[0:2, :],
                  in_=r[0:2, :], func=AF.Ln), reads=(ka,), writes=(kb,))
                S("A", lambda e, r=rb, o=r2b: e.activation(out=o[0:2, :],
                  in_=r[0:2, :], func=AF.Exp, scale=-0.5),
                  reads=(kb,), writes=(f"r2bf{p}",))
                MM(f"p{bbc}", pb[bbc][:, 0:TT], blk2t, r2b[0:2, :], True, True,
                   (f"r2bf{p}", "kconst"))
                S("V", lambda e, jj=jj, b=bbc: e.tensor_tensor(
                    out=bigv[:, jj, :], in0=pb[b][:, 0:TT], in1=eov[:, jj, :],
                    op=ALU.mult), reads=(f"eo{jj}", f"p{bbc}"), writes=(f"bg{jj}",))

            for jj in range(DT):
                bank = 2 + jj % 2
                t1 = t1fs[jj % 2]
                tk = f"t1f{jj % 2}"
                MM(f"p{bank}", pb[bank][:, 0:TT], wg2s[:, jj * 128:(jj + 1) * 128],
                   g1sb, True, True, ("g1sb", "wg2"))
                S("A", lambda e, b=bank, t=t1: e.activation(
                    out=t, in_=pb[b][:, 0:TT], func=AF.Exp, scale=-1.0),
                    reads=(f"p{bank}",), writes=(tk,))
                S("V", lambda e, t=t1: e.tensor_scalar(out=t, in0=t, scalar1=1.0,
                  scalar2=None, op0=ALU.add), reads=(), writes=(tk,))
                if j + 1 < NT:
                    ln_step(f"x{(j + 1) % 2}", xv[(j + 1) % 2], 6, 7, jj,
                            bigv[:, 8 + jj % 2, :], f"bg{8 + jj % 2}")
                if jj > 0:
                    s3_tail(jj - 1)
            for do in range(2):
                for dd in range(DT - 1):
                    MM(f"p{2 + do}", pb[2 + do][:, 0:TT],
                       wos[:, dd * D + do * 128: dd * D + do * 128 + 128],
                       bigv[:, dd, :], dd == 0, False, (f"bg{dd}", "wo"))
            s3_tail(DT - 1)
            if j + 1 < NT:
                layernorm(f"x{(j + 1) % 2}", xv[(j + 1) % 2], 6, 7,
                          stats_done=True)
            for do in range(DT):
                bank = 2 + do % 4
                for dd in (range(DT - 1, DT) if do < 2 else range(DT)):
                    MM(f"p{bank}", pb[bank][:, 0:TT],
                       wos[:, dd * D + do * 128: dd * D + do * 128 + 128],
                       bigv[:, dd, :], dd == 0 and do >= 2, dd == DT - 1,
                       (f"bg{dd}", "wo"))
                sq = sqs[do % 2]
                S("A", lambda e, b=bank, o=sq: e.copy(o, pb[b][:, 0:TT]),
                  reads=(f"p{bank}",), writes=(f"sq{do % 2}",))
                S("V", lambda e, do=do, s=sq, w=buf: e.tensor_tensor(
                    out=bigv[:, DT + do, :], in0=s, in1=xv[w][:, do, :], op=ALU.add),
                    reads=(f"sq{do % 2}", f"x{buf}"), writes=(f"bg{DT + do}",))
                S("V", lambda e, do=do, o=sq: e.tensor_tensor(
                    out=o, in0=bigv[:, DT + do, :], in1=bigv[:, DT + do, :],
                    op=ALU.mult), reads=(f"bg{DT + do}",), writes=(f"sq{do % 2}",))
                MM("p6", pb[6][0:1, 0:TT], onescol, bigv[:, DT + do, :],
                   do == 0, do == DT - 1, (f"bg{DT + do}", "kconst"))
                MM("p7", pb[7][0:1, 0:TT], onescol, sq,
                   do == 0, do == DT - 1, (f"sq{do % 2}", "kconst"))
            r0, r1, r2, r3 = rs
            S("V", lambda e: e.tensor_scalar(out=r0[0:1, :], in0=pb[6][0:1, 0:TT],
              scalar1=1.0 / D, scalar2=None, op0=ALU.mult), reads=("p6",), writes=("r0",))
            S("V", lambda e: e.tensor_scalar(out=r1[0:1, :], in0=pb[7][0:1, 0:TT],
              scalar1=1.0 / D, scalar2=None, op0=ALU.mult), reads=("p7",), writes=("r1",))
            S("V", lambda e: e.tensor_tensor(out=r2[0:1, :], in0=r0[0:1, :],
              in1=r0[0:1, :], op=ALU.mult), reads=("r0",), writes=("r2",))
            S("V", lambda e: e.scalar_tensor_tensor(out=r2[0:1, :], in0=r1[0:1, :],
              scalar=1e-5, in1=r2[0:1, :], op0=ALU.add, op1=ALU.subtract),
              reads=("r1", "r2"), writes=("r2",))
            S("A", lambda e: e.activation(out=r3[0:1, :], in_=r2[0:1, :],
              func=AF.Ln), reads=("r2",), writes=("r3",))
            S("A", lambda e, j=j: e.activation(
                out=abrow[0:1, j * TT:(j + 1) * TT], in_=r3[0:1, :],
                func=AF.Exp, scale=-0.5), reads=("r3",), writes=("abrow",))
            S("V", lambda e, j=j: e.scalar_tensor_tensor(
                out=bbrow[0:1, j * TT:(j + 1) * TT], in0=r0[0:1, :],
                scalar=-1.0, in1=abrow[0:1, j * TT:(j + 1) * TT],
                op0=ALU.mult, op1=ALU.mult),
                reads=("r0", "abrow"), writes=("bbrow",))
            x1v = big2[:, DT * TT:2 * DT * TT].rearrange("p (c t) -> p c t", c=DT)
            dstp = x1h.rearrange("(c p) t -> p c t", p=128)[:, :, j * TT:(j + 1) * TT]
            S("A", lambda e, d=dstp, s=x1v: e.dma_start(out=d, in_=s),
              reads=tuple(f"bg{DT + q}" for q in range(DT)),
              writes=(f"x1hbm{j}",), dma=True)

        sched.barrier()

        wgv = wgall.rearrange("p (k o) -> p k o", k=HT)
        wuv = wuall.rearrange("p (k o) -> p k o", k=HT)
        wdv = wdall.rearrange("p (k o) -> p k o", k=HT)
        for k in range(HT):
            src = wgt_h[k * D:(k + 1) * D, :].rearrange("(c p) hh -> p c hh", p=128)
            dst = wgv[:, k, :].rearrange("p (c hh) -> p c hh", c=DT)
            S("G", lambda e, d=dst, s=src: e.dma_start(out=d, in_=s),
              writes=(f"wg{k}",), dma=True)
            src = wut_h[k * D:(k + 1) * D, :].rearrange("(c p) hh -> p c hh", p=128)
            dst = wuv[:, k, :].rearrange("p (c hh) -> p c hh", c=DT)
            S("G", lambda e, d=dst, s=src: e.dma_start(out=d, in_=s),
              writes=(f"wu{k}",), dma=True)
            src = wdt_h[k * 128:(k + 1) * 128, :]
            S("G", lambda e, d=wdv[:, k, :], s=src: e.dma_start(out=d, in_=s),
              writes=(f"wd{k}",), dma=True)

        x1tv = [x1t[i].rearrange("p (c t) -> p c t", c=DT) for i in range(2)]
        x1fvs = [x1fs[i].rearrange("p (c t) -> p c t", c=DT) for i in range(2)]

        def b_start(kt):
            buf = kt % 2
            srcp = x1h.rearrange("(c p) t -> p c t", p=128)[:, :, kt * FT:(kt + 1) * FT]
            S("S", lambda e, d=x1tv[buf], s=srcp: e.dma_start(out=d, in_=s),
              reads=(f"x1hbm{kt * FT // TT}",), writes=(f"x1t{buf}",), dma=True)
            S("A", lambda e, b=buf: e.copy(x1fs[b], x1t[b]),
              reads=(f"x1t{buf}",), writes=(f"x1f{buf}",))
            MM("p6", pb[6][:, 0:FT], onesrow, abrow[0:1, kt * FT:(kt + 1) * FT],
               True, True, ("abrow", "kconst"))
            MM("p6", pb[6][:, FT:2 * FT], onesrow, bbrow[0:1, kt * FT:(kt + 1) * FT],
               True, True, ("bbrow", "kconst"))
            S("A", lambda e: e.copy(bcab, pb[6][:, 0:2 * FT]),
              reads=("p6",), writes=("bcab",))
            for dd in range(DT):
                S("V", lambda e, dd=dd, b=buf: e.tensor_tensor(
                    out=x1tv[b][:, dd, :], in0=x1tv[b][:, dd, :],
                    in1=bcab[:, 0:FT], op=ALU.mult),
                    reads=("bcab", f"x1f{buf}"), writes=(f"xl2{buf}{dd}",))
                S("V", lambda e, dd=dd, b=buf: e.tensor_tensor(
                    out=x1tv[b][:, dd, :], in0=x1tv[b][:, dd, :],
                    in1=bcab[:, FT:2 * FT], op=ALU.add),
                    reads=("bcab",), writes=(f"xl2{buf}{dd}",))

        b_start(0)
        for kt in range(NBT):
            buf = kt % 2

            def down_mms(ht):
                for do in range(DT):
                    ob = do // 2
                    MM(f"p{ob}", pb[ob][:, (do % 2) * FT:(do % 2) * FT + FT],
                       wdv[:, ht, do * 128:(do + 1) * 128], hb[ht % 2],
                       ht == 0 and do % 2 == 0, ht == HT - 1,
                       (f"hb{ht % 2}", f"wd{ht}"))

            for ht in range(HT):
                bank = 4 + ht % 2
                for dd in range(DT):
                    MM(f"p{bank}", pb[bank][:, 0:FT],
                       wgv[:, ht, dd * 128:(dd + 1) * 128], x1tv[buf][:, dd, :],
                       dd == 0, dd == DT - 1, (f"xl2{buf}{dd}", f"wg{ht}"))
                for dd in range(DT):
                    MM(f"p{bank}", pb[bank][:, FT:2 * FT],
                       wuv[:, ht, dd * 128:(dd + 1) * 128], x1tv[buf][:, dd, :],
                       dd == 0, dd == DT - 1, (f"xl2{buf}{dd}", f"wu{ht}"))
                S("A", lambda e, b=bank, sb=silubs[ht % 2]: e.activation(
                    out=sb, in_=pb[b][:, 0:FT], func=AF.Silu, scale=SC),
                    reads=(f"p{bank}",), writes=(f"silub{ht % 2}",))
                S("V", lambda e, b=bank, w=ht % 2, sb=silubs[ht % 2]:
                    e.tensor_tensor(
                    out=hb[w], in0=pb[b][:, FT:2 * FT], in1=sb, op=ALU.mult),
                    reads=(f"p{bank}", f"silub{ht % 2}"), writes=(f"hb{ht % 2}",))
                if ht > 0:
                    down_mms(ht - 1)
                if ht == HT - 8 and kt + 1 < NBT:
                    b_start(kt + 1)
            down_mms(HT - 1)
            for do in range(DT):
                ob = do // 2
                S("V", lambda e, do=do, ob=ob, b=buf: e.scalar_tensor_tensor(
                    out=x1fvs[b][:, do, :],
                    in0=pb[ob][:, (do % 2) * FT:(do % 2) * FT + FT],
                    scalar=SC2, in1=x1fvs[b][:, do, :], op0=ALU.mult, op1=ALU.add),
                    reads=(f"p{ob}",), writes=(f"x1f{buf}",))
            dstp = outT.rearrange("(c p) t -> p c t", p=128)[:, :, kt * FT:(kt + 1) * FT]
            S("S", lambda e, d=dstp, s=x1fvs[buf]: e.dma_start(out=d, in_=s),
              reads=(f"x1f{buf}",), writes=(f"outhbm{kt}",), dma=True)

        with nc.Block() as block:
            def emit_stream(ename):
                def body(eng):
                    for (waits, fn, sem, amt) in sched.stream[ename]:
                        for (s, v) in waits:
                            eng.wait_ge(sems[s], v)
                        inst = fn(eng)
                        inst.then_inc(sems[sem], amt)
                    if ename == "S":
                        for s, v in sorted(sched.counts.items()):
                            eng.wait_ge(sems[s], v)
                return body

            block.tensor(emit_stream("T"))
            block.vector(emit_stream("V"))
            block.scalar(emit_stream("A"))
            block.gpsimd(emit_stream("G"))
            block.sync(emit_stream("S"))

    return nc


def _prep_inputs(x, wq, wk, wv, wo, wg1, wg2, norm_w, w_gate, w_up, w_down,
                 nown, npre, ncores, nseq):
    bf = ml_dtypes.bfloat16
    idx = np.arange(D, dtype=np.float64)
    theta = BASEF ** (-2.0 * idx / D)
    t = np.arange(nseq, dtype=np.float64)
    ang = np.outer(t, theta)
    cosv = np.cos(ang)
    sinv = np.sin(ang)
    cosT = cosv.T
    sinT = sinv.T

    ident = np.eye(128, dtype=bf)
    onescol = np.ones((128, 1), dtype=bf)
    onesrow = np.ones((1, 128), dtype=bf)
    tril = np.triu(np.ones((128, 128), dtype=np.float32))
    mask4 = np.ascontiguousarray(np.tile(tril, (1, 4))).astype(bf)
    blk2 = np.zeros((128, 2), dtype=np.float32)
    blk2[0:64, 0] = 1.0
    blk2[64:128, 1] = 1.0
    blk2t = np.ascontiguousarray(blk2.T).astype(bf)
    blk2 = blk2.astype(bf)
    onescan = np.ones((128, TT), dtype=bf)
    onesone = np.ones((1, 1), dtype=bf)
    normc = np.ascontiguousarray(
        np.asarray(norm_w, np.float32).reshape(DT, 128).T)

    wgT = np.ascontiguousarray(np.asarray(w_gate).T)
    wuT = np.ascontiguousarray(np.asarray(w_up).T)
    wgt = np.ascontiguousarray(
        wgT.reshape(D, HT, 128).transpose(1, 0, 2)).reshape(HT * D, 128)
    wut = np.ascontiguousarray(
        wuT.reshape(D, HT, 128).transpose(1, 0, 2)).reshape(HT * D, 128)
    wdt = np.ascontiguousarray(np.asarray(w_down).T)

    common = dict(
        wq=np.asarray(wq, np.float32), wk=np.asarray(wk, np.float32),
        wv=np.asarray(wv, np.float32), wo=np.asarray(wo, np.float32),
        wg1=np.asarray(wg1, np.float32), wg2=np.asarray(wg2, np.float32),
        normc=normc, ident=ident, onescol=onescol, onesrow=onesrow,
        mask4=mask4, blk2=blk2, blk2t=blk2t, onescan=onescan, onesone=onesone,
        wgt=wgt.astype(np.int32), wut=wut.astype(np.int32),
        wdt=wdt.astype(np.int32),
    )

    x = np.asarray(x, np.float32)
    in_maps = []
    for c in range(ncores):
        b = c // 2
        half = c % 2
        xtb = np.ascontiguousarray(x[b].T)
        off = half * nown
        m = dict(common)
        m["xo"] = np.ascontiguousarray(xtb[:, off:off + nown])
        m["xp"] = np.ascontiguousarray(xtb[:, 0:npre])
        m["cto"] = np.ascontiguousarray(cosT[:, off:off + nown]).astype(bf)
        m["sto"] = np.ascontiguousarray(sinT[:, off:off + nown]).astype(bf)
        m["ctp"] = np.ascontiguousarray(cosv[0:npre, :]).astype(bf)
        m["stp"] = np.ascontiguousarray(sinv[0:npre, :]).astype(bf)
        m["maskb"] = np.full((128, 1), float(half), dtype=np.float32)
        in_maps.append(m)
    return in_maps


def kernel(L0_attn_x, L0_attn_wq, L0_attn_wk, L0_attn_wv, L0_attn_wo,
           L0_attn_wg1, L0_attn_wg2, L0_attn_norm_w,
           L0_ffn_w_gate, L0_ffn_w_up, L0_ffn_w_down):
    x = np.asarray(L0_attn_x, np.float32)
    nb, nseq, d = x.shape
    assert d == D
    nown = nseq // 2
    npre = nown
    ncores = 2 * nb
    in_maps = _prep_inputs(x, L0_attn_wq, L0_attn_wk, L0_attn_wv, L0_attn_wo,
                           L0_attn_wg1, L0_attn_wg2, L0_attn_norm_w,
                           L0_ffn_w_gate, L0_ffn_w_up, L0_ffn_w_down,
                           nown, npre, ncores, nseq)
    nc = build_program(nown, npre)
    res = run_bass_kernel_spmd(nc, in_maps, core_ids=list(range(ncores)))
    out = np.empty((nb, nseq, D), dtype=np.float32)
    for c in range(ncores):
        b = c // 2
        half = c % 2
        out[b, half * nown:(half + 1) * nown, :] = res.results[c]["outT"].T
    return out


# version 33
# speedup vs baseline: 1.0162x; 1.0162x over previous
import numpy as np
import ml_dtypes
from contextlib import ExitStack

import concourse.bass as bass
import concourse.mybir as mybir
from concourse.bass_utils import run_bass_kernel_spmd

F32 = mybir.dt.float32
BF16 = mybir.dt.bfloat16
I32 = mybir.dt.int32
AF = mybir.ActivationFunctionType
ALU = mybir.AluOpType

D = 1024
DT = 8
H = 16
HD = 64
TT = 512
CH = 128
NCH = TT // CH
FT = 256
HFF = 3072
HT = HFF // 128
BASEF = 10000.0
SC = 1.0 / 127.0
SC2 = SC * SC

ENGS = ("T", "V", "A", "G", "S")


class Sched:

    DMA_POOL = 8

    def __init__(self):
        self.stream = {e: [] for e in ENGS}
        self.counts = {}
        self.lastw = {}
        self.readers = {}
        self.wm = {e: {} for e in ENGS}
        self.pend = {e: [] for e in ENGS}
        self.dmacnt = {e: 0 for e in ENGS}

    def op(self, eng, fn, reads=(), writes=(), dma=False):
        deps = []
        for k in reads:
            t = self.lastw.get(k)
            if t is not None:
                deps.append(t)
        for k in writes:
            t = self.lastw.get(k)
            if t is not None:
                deps.append(t)
            deps.extend(self.readers.get(k, ()))
        deps.extend(self.pend[eng])
        self.pend[eng] = []
        if dma:
            i = self.dmacnt[eng]
            self.dmacnt[eng] = i + 1
            sem = f"d{eng}{i % self.DMA_POOL}"
            cyc = i // self.DMA_POOL
            if cyc > 0:
                deps.append((sem, 16 * cyc))
            amt = 16
            tokv = 16 * (cyc + 1)
        else:
            sem = "s" + eng
            amt = 1
            tokv = self.counts.get(sem, 0) + 1
        byse = {}
        for (s, v) in deps:
            if s == "sT" and eng == "T":
                continue
            if self.wm[eng].get(s, 0) >= v:
                continue
            byse[s] = max(byse.get(s, 0), v)
        waits = sorted(byse.items())
        for (s, v) in waits:
            self.wm[eng][s] = v
        self.counts[sem] = tokv
        tok = (sem, tokv)
        self.stream[eng].append((waits, fn, sem, amt))
        for k in reads:
            self.readers.setdefault(k, []).append(tok)
        for k in writes:
            self.lastw[k] = tok
            self.readers[k] = []
        return tok

    def barrier(self):
        toks = list(self.counts.items())
        for e in ENGS:
            self.pend[e] = list(toks)


def build_program(nown, npre):
    NT = nown // TT
    NPT = npre // TT
    NBT = nown // FT

    nc = bass.Bass("TRN2", target_bir_lowering=False, debug=False)

    def din(name, shape, dt=F32):
        return nc.dram_tensor(name, list(shape), dt, kind="ExternalInput").ap()

    xo = din("xo", [D, nown])
    xp = din("xp", [D, npre])
    cto = din("cto", [D, nown], BF16)
    sto = din("sto", [D, nown], BF16)
    ctp = din("ctp", [npre, D], BF16)
    stp = din("stp", [npre, D], BF16)
    wq_h = din("wq", [D, D])
    wk_h = din("wk", [D, D])
    wv_h = din("wv", [D, D])
    wo_h = din("wo", [D, D])
    wg1_h = din("wg1", [D, HD])
    wg2_h = din("wg2", [HD, D])
    normc_h = din("normc", [128, DT], F32)
    maskb_h = din("maskb", [128, 1], F32)
    ident_h = din("ident", [128, 128], BF16)
    onescol_h = din("onescol", [128, 1], BF16)
    onesrow_h = din("onesrow", [1, 128], BF16)
    mask4_h = din("mask4", [128, 4 * CH], BF16)
    blk2_h = din("blk2", [128, 2], BF16)
    blk2t_h = din("blk2t", [2, 128], BF16)
    onescan_h = din("onescan", [128, TT], BF16)
    onesone_h = din("onesone", [1, 1], BF16)
    wgt_h = din("wgt", [HT * D, 128], I32)
    wut_h = din("wut", [HT * D, 128], I32)
    wdt_h = din("wdt", [HFF, D], I32)

    x1h = nc.dram_tensor("x1h", [D, nown], BF16, kind="Internal").ap()
    outT = nc.dram_tensor("outT", [D, nown], F32, kind="ExternalOutput").ap()

    sched = Sched()
    S = sched.op

    with ExitStack() as ctx:
        NBF = 96688
        NF = 4632
        arena = ctx.enter_context(nc.sbuf_tensor("arena", [128, NBF], BF16))
        arenaf = ctx.enter_context(nc.sbuf_tensor("arenaf", [128, NF], F32))

        cur = [0]

        def abf(nelem, rows=128):
            o = cur[0]
            assert o + nelem * 2 <= NBF * 2, "bf arena overflow"
            cur[0] += ((nelem * 2 + 31) // 32) * 32
            return arena[0:rows, o // 2: o // 2 + nelem]

        curf = [0]

        def af32(nelem, rows=128):
            o = curf[0]
            assert o + nelem * 4 <= NF * 4, "f32 arena overflow"
            curf[0] += ((nelem * 4 + 31) // 32) * 32
            return arenaf[0:rows, o // 4: o // 4 + nelem]

        ident = abf(128)
        onescol = abf(1)
        onesrow = abf(128, rows=1)
        mask4 = abf(4 * CH)
        blk2 = abf(2)
        blk2t = abf(128, rows=2)
        onescan = abf(TT)
        onesone = abf(1, rows=1)
        maskb = af32(1)
        normc = af32(DT)
        nsb = abf(H * HD)
        abrow = abf(nown, rows=1)
        bbrow = abf(nown, rows=1)
        zrun = af32(DT)
        pers_end = (cur[0], curf[0])

        xb = [abf(DT * TT), abf(DT * TT)]
        ct = abf(DT * TT)
        st = abf(DT * TT)
        xln = abf(DT * TT)
        eo = abf(DT * TT)
        big2 = abf(2 * DT * TT)
        qh = abf(H * TT)
        vb = [abf(D), abf(D)]
        etok = abf(2 * D)
        scbs = [abf(4 * CH), abf(4 * CH)]
        qpres = [abf(TT), abf(TT)]
        sqs = [abf(TT), abf(TT)]
        g1sb = abf(TT, rows=64)
        rba = abf(TT, rows=1)
        rbb = abf(TT, rows=1)
        r2bfs = [abf(TT, rows=2), abf(TT, rows=2)]
        bca = abf(TT)
        bcb = abf(TT)
        wqs = abf(DT * D)
        wks = abf(DT * D)
        wvs = abf(DT * D)
        wos = abf(DT * D)
        wg1s = abf(DT * HD)
        wg2s = abf(D, rows=64)
        t1fs = [af32(TT), af32(TT)]
        zf = [af32(TT), af32(TT)]
        rs = [af32(TT, rows=2) for _ in range(4)]
        a_end = (cur[0], curf[0])

        cur[0], curf[0] = pers_end
        wgall = abf(HT * D)
        wuall = abf(HT * D)
        wdall = abf(HT * D)
        x1t = [abf(DT * FT), abf(DT * FT)]
        hb = [abf(FT), abf(FT)]
        bcab = abf(2 * FT)
        x1fs = [af32(DT * FT), af32(DT * FT)]
        silubs = [af32(FT), af32(FT)]
        b_end = (cur[0], curf[0])
        assert max(a_end[0], b_end[0]) <= NBF * 2, (a_end, b_end)
        assert max(a_end[1], b_end[1]) <= NF * 4, (a_end, b_end)

        sebf = qh[0:1, 0:D]
        xv = [xb[i].rearrange("p (c t) -> p c t", c=DT) for i in range(2)]
        ctv = ct.rearrange("p (c t) -> p c t", c=DT)
        stv = st.rearrange("p (c t) -> p c t", c=DT)
        ctok = ct.rearrange("p (c t) -> p c t", c=NCH)[:, :, 0:D]
        stok = st.rearrange("p (c t) -> p c t", c=NCH)[:, :, 0:D]
        eov = eo.rearrange("p (c t) -> p c t", c=DT)
        bigv = big2.rearrange("p (c t) -> p c t", c=2 * DT)

        pbk = [ctx.enter_context(nc.psum_tensor(f"pb{i}", [128, 512], F32))
               for i in range(8)]
        pb = pbk
        pbb = [pbk[i].bitcast(BF16) for i in range(8)]

        sems = {}
        semnames = ["sT", "sV", "sA", "sG"]
        for q in ("S", "G", "A"):
            for r in range(Sched.DMA_POOL):
                semnames.append(f"d{q}{r}")
        for s in semnames:
            sems[s] = ctx.enter_context(nc.semaphore(s))

        for (dst, src) in ((ident, ident_h), (onescol, onescol_h),
                           (onesrow, onesrow_h), (mask4, mask4_h),
                           (blk2, blk2_h), (blk2t, blk2t_h),
                           (onescan, onescan_h), (onesone, onesone_h),
                           (maskb, maskb_h), (normc, normc_h)):
            S("S", lambda e, d=dst, s=src: e.dma_start(out=d, in_=s[:, :]),
              writes=("kconst",), dma=True)

        def wload(dst, src, key):
            r = src.rearrange("(c p) o -> p c o", p=128)
            nco = r.shape[1]
            d3 = dst.rearrange("p (c o) -> p c o", c=nco)
            S("G", lambda e, d=d3, s=r: e.dma_start(out=d, in_=s),
              writes=(key,), dma=True)

        wload(wks, wk_h, "wk")
        wload(wvs, wv_h, "wv")

        def loadx(src, j, buf):
            s3 = src.rearrange("(c p) t -> p c t", p=128)[:, :, j * TT:(j + 1) * TT]
            S("G", lambda e, d=xv[buf], s=s3: e.dma_start(out=d, in_=s),
              writes=(f"x{buf}",), dma=True)

        def loadtab_feat(j):
            c3 = cto.rearrange("(c p) t -> p c t", p=128)[:, :, j * TT:(j + 1) * TT]
            s3 = sto.rearrange("(c p) t -> p c t", p=128)[:, :, j * TT:(j + 1) * TT]
            S("S", lambda e, d=ctv, s=c3: e.dma_start(out=d, in_=s), writes=("ct",), dma=True)
            S("S", lambda e, d=stv, s=s3: e.dma_start(out=d, in_=s), writes=("st",), dma=True)

        def loadtab_tok(j):
            c3 = ctp[j * TT:(j + 1) * TT, :].rearrange("(c p) e -> p c e", p=128)
            s3 = stp[j * TT:(j + 1) * TT, :].rearrange("(c p) e -> p c e", p=128)
            S("S", lambda e, d=ctok, s=c3: e.dma_start(out=d, in_=s), writes=("ct",), dma=True)
            S("S", lambda e, d=stok, s=s3: e.dma_start(out=d, in_=s), writes=("st",), dma=True)

        def MM(bank, out, lhsT, rhs, start, stop, reads):
            S("T", lambda e, o=out, l=lhsT, r=rhs, s=start, p=stop:
              e.matmul(o, l, r, start=s, stop=p), reads=reads, writes=(bank,))

        def ACOPY(out, in_, reads, writes):
            S("A", lambda e, o=out, i=in_: e.copy(o, i), reads=reads,
              writes=writes)

        def ln_step(xkey, xvw, b0, b1, dd, sqbuf, sqkey, eng="V"):
            S(eng, lambda e, o=sqbuf, a=xvw[:, dd, :]:
              e.tensor_tensor(out=o, in0=a, in1=a, op=ALU.mult),
              reads=(xkey,), writes=(sqkey,))
            MM(f"p{b0}", pb[b0][0:1, 0:TT], onescol, xvw[:, dd, :],
               dd == 0, dd == DT - 1, (xkey, "kconst"))
            MM(f"p{b1}", pb[b1][0:1, 0:TT], onescol, sqbuf,
               dd == 0, dd == DT - 1, (sqkey, "kconst"))

        def layernorm(xkey, xvw, b0, b1, stats_done=False):
            if not stats_done:
                for dd in range(DT):
                    ln_step(xkey, xvw, b0, b1, dd, sqs[dd % 2], f"sq{dd % 2}")
            r0, r1, r2, r3 = rs
            S("V", lambda e: e.tensor_scalar(out=r0[0:1, :], in0=pb[b0][0:1, 0:TT],
              scalar1=1.0 / D, scalar2=None, op0=ALU.mult),
              reads=(f"p{b0}",), writes=("r0",))
            S("V", lambda e: e.tensor_scalar(out=r1[0:1, :], in0=pb[b1][0:1, 0:TT],
              scalar1=1.0 / D, scalar2=None, op0=ALU.mult),
              reads=(f"p{b1}",), writes=("r1",))
            S("V", lambda e: e.tensor_tensor(out=r2[0:1, :], in0=r0[0:1, :],
              in1=r0[0:1, :], op=ALU.mult), reads=("r0",), writes=("r2",))
            S("V", lambda e: e.scalar_tensor_tensor(out=r2[0:1, :], in0=r1[0:1, :],
              scalar=1e-5, in1=r2[0:1, :], op0=ALU.add, op1=ALU.subtract),
              reads=("r1", "r2"), writes=("r2",))
            S("A", lambda e: e.activation(out=r3[0:1, :], in_=r2[0:1, :],
              func=AF.Ln), reads=("r2",), writes=("r3",))
            S("A", lambda e: e.activation(out=rba[0:1, :], in_=r3[0:1, :],
              func=AF.Exp, scale=-0.5), reads=("r3",), writes=("rba",))
            S("V", lambda e: e.scalar_tensor_tensor(out=rbb[0:1, :], in0=r0[0:1, :],
              scalar=-1.0, in1=rba[0:1, :], op0=ALU.mult, op1=ALU.mult),
              reads=("r0", "rba"), writes=("rbb",))
            MM(f"p{b0}", pb[b0][:, 0:TT], onesrow, rba[0:1, :], True, True,
               ("rba", "kconst"))
            MM(f"p{b1}", pb[b1][:, 0:TT], onesrow, rbb[0:1, :], True, True,
               ("rbb", "kconst"))
            S("A", lambda e: e.copy(bca, pb[b0][:, 0:TT]),
              reads=(f"p{b0}",), writes=("bca",))
            S("A", lambda e: e.copy(bcb, pb[b1][:, 0:TT]),
              reads=(f"p{b1}",), writes=("bcb",))
            xlv = xln.rearrange("p (c t) -> p c t", c=DT)
            for dd in range(DT):
                S("V", lambda e, d=xlv[:, dd, :], a=xvw[:, dd, :]:
                  e.tensor_tensor(out=d, in0=a, in1=bca, op=ALU.mult),
                  reads=(xkey, "bca"), writes=(f"xl{dd}",))
                S("V", lambda e, d=xlv[:, dd, :]:
                  e.tensor_tensor(out=d, in0=d, in1=bcb, op=ALU.add),
                  reads=("bcb",), writes=(f"xl{dd}",))
            return xlv

        first_n = [True]
        for jp in range(NPT):
            buf = jp % 2
            loadx(xp, jp, buf)
            loadtab_tok(jp)
            if jp == 1 or (NPT == 1 and jp == 0):
                wload(wqs, wq_h, "wq")
                wload(wos, wo_h, "wo")
                wload(wg1s, wg1_h, "wg1")
                S("G", lambda e: e.dma_start(out=wg2s, in_=wg2_h[:, :]),
                  writes=("wg2",), dma=True)
                for dd in range(DT):
                    S("V", lambda e, dd=dd: e.tensor_scalar(
                        out=wos[:, dd * D:(dd + 1) * D],
                        in0=wos[:, dd * D:(dd + 1) * D],
                        scalar1=normc[:, dd:dd + 1], scalar2=None, op0=ALU.mult),
                        reads=("kconst",), writes=("wo",))
            xlv = layernorm(f"x{buf}", xv[buf], 2, 3)

            def pre_tail(cc, jp=jp):
                gfirst = (jp == 0 and cc == 0)
                glast_pre = (jp == NPT - 1 and cc == NCH - 1)
                MM("p6", pb[6][0:1, 0:512], onescol, eo[:, 0:512],
                   gfirst, glast_pre, ("eo0", "kconst"))
                MM("p7", pb[7][0:1, 0:512], onescol, eo[:, 512:1024],
                   gfirst, glast_pre, ("eo1", "kconst"))
                for h in range(H):
                    bank = h // 8
                    MM(f"p{bank}", pb[bank][:, (h % 8) * HD:(h % 8) * HD + HD],
                       etok[:, h * 128:(h + 1) * 128],
                       vb[cc % 2][:, h * HD:(h + 1) * HD],
                       first_n[0] and h % 8 == 0, h % 8 == 7,
                       (f"etok{h // 4}", f"vb{cc % 2}"))
                first_n[0] = False

            for cc in range(NCH):
                vbuf = cc % 2
                for half in range(2):
                    bank = 2 + half
                    for dd in range(DT):
                        MM(f"p{bank}", pb[bank][:, 0:512],
                           xlv[:, dd, cc * CH:(cc + 1) * CH],
                           wks[:, dd * D + half * 512: dd * D + half * 512 + 512],
                           dd == 0, dd == DT - 1, (f"xl{dd}", "wk"))
                for half in range(2):
                    bank = 4 + half
                    for dd in range(DT):
                        MM(f"p{bank}", pb[bank][:, 0:512],
                           xlv[:, dd, cc * CH:(cc + 1) * CH],
                           wvs[:, dd * D + half * 512: dd * D + half * 512 + 512],
                           dd == 0, dd == DT - 1, (f"xl{dd}", "wv"))
                if cc > 0:
                    pre_tail(cc - 1)
                for half in range(2):
                    bank = 2 + half
                    S("A", lambda e, b=bank, h=half: e.activation(
                        out=eo[:, h * 512:(h + 1) * 512], in_=pb[b][:, 0:512],
                        func=AF.Exp), reads=(f"p{bank}",), writes=(f"eo{half}",))
                S("V", lambda e: e.tensor_scalar(out=eo[:, 0:D], in0=eo[:, 0:D],
                  scalar1=maskb, scalar2=None, op0=ALU.mult),
                  reads=("kconst",), writes=("eo0", "eo1"))
                ev = eo[:, 0:D].rearrange("p (h e) -> p h e", h=H)
                tv = etok.rearrange("p (h e) -> p h e", h=H)
                S("V", lambda e, cc=cc: e.tensor_tensor(
                    out=tv[:, :, 0:HD], in0=ev,
                    in1=ctok[:, cc, :].rearrange("p (h e) -> p h e", h=H),
                    op=ALU.mult), reads=("eo0", "eo1", "ct"),
                    writes=("etok0", "etok1", "etok2", "etok3"))
                S("V", lambda e, cc=cc: e.tensor_tensor(
                    out=tv[:, :, HD:128], in0=ev,
                    in1=stok[:, cc, :].rearrange("p (h e) -> p h e", h=H),
                    op=ALU.mult), reads=("eo0", "eo1", "st"),
                    writes=("etok0", "etok1", "etok2", "etok3"))
                for half in range(2):
                    bank = 4 + half
                    S("A", lambda e, b=bank, h=half, w=vbuf: e.copy(
                        vb[w][:, h * 512:(h + 1) * 512], pb[b][:, 0:512]),
                        reads=(f"p{bank}",), writes=(f"vb{vbuf}",))
            pre_tail(NCH - 1)

        if NPT > 0:
            S("A", lambda e: e.copy(sebf[0:1, 0:512], pb[6][0:1, 0:512]),
              reads=("p6",), writes=("qh0",))
            S("A", lambda e: e.copy(sebf[0:1, 512:1024], pb[7][0:1, 0:512]),
              reads=("p7",), writes=("qh1",))
            for jj in range(DT):
                MM("p2", pb[2][:, jj:jj + 1], sebf[0:1, jj * 128:(jj + 1) * 128],
                   onesone, jj == 0, jj == DT - 1, ("qh0", "qh1", "kconst"))
            S("V", lambda e: e.tensor_scalar(out=zrun[:, 0:DT], in0=pb[2][:, 0:DT],
              scalar1=1.0, scalar2=None, op0=ALU.add), reads=("p2",),
              writes=tuple(f"zrun{q}" for q in range(DT)))
            S("A", lambda e: e.copy(nsb[:, 0:512], pb[0][:, 0:512]),
              reads=("p0",), writes=("nsb",))
            S("A", lambda e: e.copy(nsb[:, 512:1024], pb[1][:, 0:512]),
              reads=("p1",), writes=("nsb",))
        else:
            S("V", lambda e: e.memset(zrun[:, 0:DT], 1.0),
              writes=tuple(f"zrun{q}" for q in range(DT)))
            S("V", lambda e: e.memset(nsb, 0.0), writes=("nsb",))

        loadx(xo, 0, 0)
        loadtab_feat(0)
        xlv = layernorm("x0", xv[0], 6, 7)
        for j in range(NT):
            buf = j % 2

            def s1_tail(jj):
                bq = 4 + (2 * jj + 1) % 4
                t1 = t1fs[jj % 2]
                tk = f"t1f{jj % 2}"
                qp = qpres[jj % 2]
                qk = f"qpre{jj % 2}"
                zb = zf[jj % 2]
                S("A", lambda e, z=zb, t=t1: e.activation(out=z, in_=t, func=AF.Ln),
                  reads=(tk,), writes=(f"zf{jj % 2}",))
                S("A", lambda e, z=zb: e.activation(out=z, in_=z, func=AF.Exp,
                  scale=-1.0), reads=(), writes=(f"zf{jj % 2}",))
                S("V", lambda e, b=bq, z=zb, q=qp: e.tensor_tensor(
                    out=q, in0=pb[b][:, 0:TT], in1=z, op=ALU.mult),
                    reads=(f"p{bq}", f"zf{jj % 2}"), writes=(qk,))
                for hh in range(2):
                    h = 2 * jj + hh
                    r0p, r1p = hh * 64, hh * 64 + 64
                    for (src, skey, dstv, dkey) in (
                            (qp, qk, qh, f"qh{h}"),
                            (eov[:, jj, :], f"eo{jj}", big2, f"bg{h}")):
                        base = h * TT
                        S("V", lambda e, s=src, d=dstv, b=base, r0=r0p, r1=r1p, jj=jj:
                          e.tensor_tensor(out=d[0:64, b:b + TT], in0=s[r0:r1, :],
                                          in1=ctv[r0:r1, jj, :], op=ALU.mult),
                          reads=(skey, "ct"), writes=(dkey,))
                        S("V", lambda e, s=src, d=dstv, b=base, r0=r0p, r1=r1p, jj=jj:
                          e.tensor_tensor(out=d[64:128, b:b + TT], in0=s[r0:r1, :],
                                          in1=stv[r0:r1, jj, :], op=ALU.mult),
                          reads=(skey, "st"), writes=(dkey,))

            for jj in range(DT):
                bank = 4 + (2 * jj) % 4
                bq = 4 + (2 * jj + 1) % 4
                t1 = t1fs[jj % 2]
                tk = f"t1f{jj % 2}"
                zb = zf[jj % 2]
                for dd in range(DT):
                    MM(f"p{bank}", pb[bank][:, 0:TT],
                       wks[:, dd * D + jj * 128: dd * D + jj * 128 + 128],
                       xlv[:, dd, :], dd == 0, dd == DT - 1, (f"xl{dd}", "wk"))
                S("A", lambda e, b=bank, jj=jj: e.activation(
                    out=eov[:, jj, :], in_=pb[b][:, 0:TT], func=AF.Exp),
                    reads=(f"p{bank}",), writes=(f"eo{jj}",))
                S("V", lambda e, z=zb, jj=jj: e.tensor_tensor_scan(
                    out=z, data0=onescan, data1=eov[:, jj, :],
                    initial=zrun[:, jj:jj + 1], op0=ALU.mult, op1=ALU.add),
                    reads=(f"eo{jj}", f"zrun{jj}", "kconst"), writes=(f"zf{jj % 2}",))
                S("V", lambda e, z=zb, jj=jj: e.tensor_copy(
                    out=zrun[:, jj:jj + 1], in_=z[:, TT - 1:TT]),
                    reads=(f"zf{jj % 2}",), writes=(f"zrun{jj}",))
                for dd in range(DT):
                    MM(f"p{bq}", pb[bq][:, 0:TT],
                       wqs[:, dd * D + jj * 128: dd * D + jj * 128 + 128],
                       xlv[:, dd, :], dd == 0, dd == DT - 1, (f"xl{dd}", "wq"))
                S("A", lambda e, b=bq, t=t1: e.activation(
                    out=t, in_=pb[b][:, 0:TT], func=AF.Exp, scale=-1.0),
                    reads=(f"p{bq}",), writes=(tk,))
                S("V", lambda e, z=zb, t=t1: e.scalar_tensor_tensor(
                    out=t, in0=t, scalar=1.0, in1=z,
                    op0=ALU.add, op1=ALU.mult),
                    reads=(f"zf{jj % 2}",), writes=(tk,))
                if jj > 0:
                    s1_tail(jj - 1)
            s1_tail(DT - 1)
            for dd in range(DT):
                MM("p2", pb[2][0:64, 0:TT], wg1s[:, dd * HD:(dd + 1) * HD],
                   xlv[:, dd, :], dd == 0, dd == DT - 1, (f"xl{dd}", "wg1"))
            S("A", lambda e: e.copy(g1sb, pb[2][0:64, 0:TT]),
              reads=("p2",), writes=("g1sb",))

            GROUPS = [[0, 2, 4, 6], [8, 10, 12, 14], [1, 3, 5, 7], [9, 11, 13, 15]]
            POS = {}
            for g_, hs_ in enumerate(GROUPS):
                for i_, h_ in enumerate(hs_):
                    POS[h_] = g_ * 4 + i_
            def v_mms(cc):
                vbuf = cc % 2
                for half in range(2):
                    bank = 2 + half
                    for dd in range(DT):
                        MM(f"p{bank}", pb[bank][:, 0:512],
                           xlv[:, dd, cc * CH:(cc + 1) * CH],
                           wvs[:, dd * D + half * 512: dd * D + half * 512 + 512],
                           dd == 0, dd == DT - 1, (f"xl{dd}", "wv"))
                    S("A", lambda e, b=bank, h=half, w=vbuf: e.copy(
                        vb[w][:, h * 512:(h + 1) * 512], pb[b][:, 0:512]),
                        reads=(f"p{bank}",), writes=(f"vb{vbuf}",))

            for cc in range(NCH):
                g_first = (NPT == 0 and j == 0 and cc == 0)
                g_last = (j == NT - 1 and cc == NCH - 1)
                vbuf = cc % 2
                if cc == 0:
                    v_mms(0)

                def head_tail(g, cc=cc, vbuf=vbuf, g_first=g_first, g_last=g_last):
                    heads = GROUPS[g]
                    bo = 6 + g % 2
                    for i, h in enumerate(heads):
                        MM(f"p{bo}", pb[bo][0:64, i * 128:(i + 1) * 128],
                           nsb[:, h * HD:(h + 1) * HD],
                           qh[:, h * TT + cc * CH: h * TT + cc * CH + CH],
                           i == 0, False, ("nsb", f"qh{h}"))
                        MM(f"p{bo}", pb[bo][0:64, i * 128:(i + 1) * 128],
                           vb[vbuf][:, h * HD:(h + 1) * HD],
                           scbs[g % 2][:, i * 128: i * 128 + 128],
                           False, i == 3, (f"vb{vbuf}", f"scb{g % 2}"))
                    for i, h in enumerate(heads):
                        bank = h // 8
                        MM(f"p{bank}", pb[bank][:, (h % 8) * HD:(h % 8) * HD + HD],
                           etok[:, POS[h] * 128: POS[h] * 128 + 128],
                           vb[vbuf][:, h * HD:(h + 1) * HD],
                           g_first and g < 2 and i == 0,
                           g_last or (i == 3 and g >= 2),
                           (f"etok{POS[h] // 4}", f"vb{vbuf}"))
                    rr = 0 if g < 2 else 64
                    c0 = 0 if g % 2 == 0 else 4
                    src = pb[bo][0:64, 0:512].rearrange("p (c t) -> p c t", c=4)
                    dst = eov[rr:rr + 64, c0:c0 + 4, cc * CH:(cc + 1) * CH]
                    S("V", lambda e, s=src, d=dst: e.tensor_copy(out=d, in_=s),
                      reads=(f"p{bo}",),
                      writes=tuple(f"eo{c0 + q}" for q in range(4)))

                for g, heads in enumerate(GROUPS):
                    bsc = 2 + g % 2
                    btr = 4 + g % 2
                    for i, h in enumerate(heads):
                        S("T", lambda e, h=h, i=i, b=btr, cc=cc: e.transpose(
                            out=pbb[b][:, i * 128:(i + 1) * 128],
                            in_=big2[:, h * TT + cc * CH: h * TT + cc * CH + CH],
                            identity=ident),
                            reads=(f"bg{h}", "kconst"), writes=(f"p{btr}",))
                    S("A", lambda e, b=btr, g=g: e.copy(
                        etok[:, g * 512:(g + 1) * 512], pbb[b][:, 0:512]),
                        reads=(f"p{btr}",), writes=(f"etok{g}",))
                    for i, h in enumerate(heads):
                        MM(f"p{bsc}", pb[bsc][:, i * 128:(i + 1) * 128],
                           big2[:, h * TT + cc * CH: h * TT + cc * CH + CH],
                           qh[:, h * TT + cc * CH: h * TT + cc * CH + CH],
                           i == 0, i == 3, (f"bg{h}", f"qh{h}"))
                    S("V", lambda e, b=bsc, g=g: e.tensor_tensor(
                        out=scbs[g % 2][:, 0:512],
                        in0=pb[b][:, 0:512], in1=mask4, op=ALU.mult),
                        reads=(f"p{bsc}", "kconst"), writes=(f"scb{g % 2}",))
                    if g >= 1:
                        head_tail(g - 1)
                if cc + 1 < NCH:
                    v_mms(cc + 1)
                head_tail(3)
                S("V", lambda e: e.tensor_copy(out=nsb[:, 0:512], in_=pb[0][:, 0:512]),
                  reads=("p0",), writes=("nsb",))
                S("V", lambda e: e.tensor_copy(out=nsb[:, 512:1024], in_=pb[1][:, 0:512]),
                  reads=("p1",), writes=("nsb",))

            if j + 1 < NT:
                loadx(xo, j + 1, (j + 1) % 2)
                loadtab_feat(j + 1)

            def s3_tail(jj):
                bank = 2 + jj % 2
                t1 = t1fs[jj % 2]
                tk = f"t1f{jj % 2}"
                s1 = sqs[(jj + 1) % 2]
                sk = f"sq{(jj + 1) % 2}"
                p = jj % 2
                ra, ka = (rs[2], "r2") if p == 0 else (rs[0], "r0")
                rb, kb = (rs[3], "r3") if p == 0 else (rs[1], "r1")
                r2b = r2bfs[p]
                bred = 4 + p
                bbc = 5 - p
                S("A", lambda e, t=t1: e.activation(out=t, in_=t, func=AF.Ln),
                  reads=(tk,), writes=(tk,))
                S("A", lambda e, t=t1, o=s1: e.activation(out=o, in_=t, func=AF.Exp,
                  scale=-1.0), reads=(tk,), writes=(sk,))
                S("V", lambda e, jj=jj, o=s1: e.tensor_tensor(
                    out=eov[:, jj, :], in0=eov[:, jj, :], in1=o, op=ALU.mult),
                    reads=(sk,), writes=(f"eo{jj}",))
                sq = sqs[jj % 2]
                S("V", lambda e, jj=jj, o=sq: e.tensor_tensor(
                    out=o, in0=eov[:, jj, :], in1=eov[:, jj, :], op=ALU.mult),
                    reads=(f"eo{jj}",), writes=(f"sq{jj % 2}",))
                MM(f"p{bred}", pb[bred][0:2, 0:TT], blk2, sq, True, True,
                   (f"sq{jj % 2}", "kconst"))
                S("V", lambda e, b=bred, r=ra: e.tensor_scalar(out=r[0:2, :],
                  in0=pb[b][0:2, 0:TT], scalar1=1.0 / HD, scalar2=1e-6,
                  op0=ALU.mult, op1=ALU.add), reads=(f"p{bred}",), writes=(ka,))
                S("A", lambda e, r=ra, o=rb: e.activation(out=o[0:2, :],
                  in_=r[0:2, :], func=AF.Ln), reads=(ka,), writes=(kb,))
                S("A", lambda e, r=rb, o=r2b: e.activation(out=o[0:2, :],
                  in_=r[0:2, :], func=AF.Exp, scale=-0.5),
                  reads=(kb,), writes=(f"r2bf{p}",))
                MM(f"p{bbc}", pb[bbc][:, 0:TT], blk2t, r2b[0:2, :], True, True,
                   (f"r2bf{p}", "kconst"))
                S("V", lambda e, jj=jj, b=bbc: e.tensor_tensor(
                    out=bigv[:, jj, :], in0=pb[b][:, 0:TT], in1=eov[:, jj, :],
                    op=ALU.mult), reads=(f"eo{jj}", f"p{bbc}"), writes=(f"bg{jj}",))

            for jj in range(DT):
                bank = 2 + jj % 2
                t1 = t1fs[jj % 2]
                tk = f"t1f{jj % 2}"
                MM(f"p{bank}", pb[bank][:, 0:TT], wg2s[:, jj * 128:(jj + 1) * 128],
                   g1sb, True, True, ("g1sb", "wg2"))
                S("A", lambda e, b=bank, t=t1: e.activation(
                    out=t, in_=pb[b][:, 0:TT], func=AF.Exp, scale=-1.0),
                    reads=(f"p{bank}",), writes=(tk,))
                S("V", lambda e, t=t1: e.tensor_scalar(out=t, in0=t, scalar1=1.0,
                  scalar2=None, op0=ALU.add), reads=(), writes=(tk,))
                if j + 1 < NT:
                    ln_step(f"x{(j + 1) % 2}", xv[(j + 1) % 2], 6, 7, jj,
                            bigv[:, 8 + jj % 2, :], f"bg{8 + jj % 2}")
                if jj > 0:
                    s3_tail(jj - 1)
            for do in range(2):
                for dd in range(DT - 1):
                    MM(f"p{2 + do}", pb[2 + do][:, 0:TT],
                       wos[:, dd * D + do * 128: dd * D + do * 128 + 128],
                       bigv[:, dd, :], dd == 0, False, (f"bg{dd}", "wo"))
            s3_tail(DT - 1)
            if j + 1 < NT:
                layernorm(f"x{(j + 1) % 2}", xv[(j + 1) % 2], 6, 7,
                          stats_done=True)
            for do in range(DT):
                bank = 2 + do % 4
                for dd in (range(DT - 1, DT) if do < 2 else range(DT)):
                    MM(f"p{bank}", pb[bank][:, 0:TT],
                       wos[:, dd * D + do * 128: dd * D + do * 128 + 128],
                       bigv[:, dd, :], dd == 0 and do >= 2, dd == DT - 1,
                       (f"bg{dd}", "wo"))
                sq = sqs[do % 2]
                S("A", lambda e, b=bank, o=sq: e.copy(o, pb[b][:, 0:TT]),
                  reads=(f"p{bank}",), writes=(f"sq{do % 2}",))
                S("V", lambda e, do=do, s=sq, w=buf: e.tensor_tensor(
                    out=bigv[:, DT + do, :], in0=s, in1=xv[w][:, do, :], op=ALU.add),
                    reads=(f"sq{do % 2}", f"x{buf}"), writes=(f"bg{DT + do}",))
                S("V", lambda e, do=do, o=sq: e.tensor_tensor(
                    out=o, in0=bigv[:, DT + do, :], in1=bigv[:, DT + do, :],
                    op=ALU.mult), reads=(f"bg{DT + do}",), writes=(f"sq{do % 2}",))
                MM("p6", pb[6][0:1, 0:TT], onescol, bigv[:, DT + do, :],
                   do == 0, do == DT - 1, (f"bg{DT + do}", "kconst"))
                MM("p7", pb[7][0:1, 0:TT], onescol, sq,
                   do == 0, do == DT - 1, (f"sq{do % 2}", "kconst"))
            r0, r1, r2, r3 = rs
            S("V", lambda e: e.tensor_scalar(out=r0[0:1, :], in0=pb[6][0:1, 0:TT],
              scalar1=1.0 / D, scalar2=None, op0=ALU.mult), reads=("p6",), writes=("r0",))
            S("V", lambda e: e.tensor_scalar(out=r1[0:1, :], in0=pb[7][0:1, 0:TT],
              scalar1=1.0 / D, scalar2=None, op0=ALU.mult), reads=("p7",), writes=("r1",))
            S("V", lambda e: e.tensor_tensor(out=r2[0:1, :], in0=r0[0:1, :],
              in1=r0[0:1, :], op=ALU.mult), reads=("r0",), writes=("r2",))
            S("V", lambda e: e.scalar_tensor_tensor(out=r2[0:1, :], in0=r1[0:1, :],
              scalar=1e-5, in1=r2[0:1, :], op0=ALU.add, op1=ALU.subtract),
              reads=("r1", "r2"), writes=("r2",))
            S("A", lambda e: e.activation(out=r3[0:1, :], in_=r2[0:1, :],
              func=AF.Ln), reads=("r2",), writes=("r3",))
            S("A", lambda e, j=j: e.activation(
                out=abrow[0:1, j * TT:(j + 1) * TT], in_=r3[0:1, :],
                func=AF.Exp, scale=-0.5), reads=("r3",), writes=("abrow",))
            S("V", lambda e, j=j: e.scalar_tensor_tensor(
                out=bbrow[0:1, j * TT:(j + 1) * TT], in0=r0[0:1, :],
                scalar=-1.0, in1=abrow[0:1, j * TT:(j + 1) * TT],
                op0=ALU.mult, op1=ALU.mult),
                reads=("r0", "abrow"), writes=("bbrow",))
            x1v = big2[:, DT * TT:2 * DT * TT].rearrange("p (c t) -> p c t", c=DT)
            dstp = x1h.rearrange("(c p) t -> p c t", p=128)[:, :, j * TT:(j + 1) * TT]
            S("A", lambda e, d=dstp, s=x1v: e.dma_start(out=d, in_=s),
              reads=tuple(f"bg{DT + q}" for q in range(DT)),
              writes=(f"x1hbm{j}",), dma=True)

        sched.barrier()

        wgv = wgall.rearrange("p (k o) -> p k o", k=HT)
        wuv = wuall.rearrange("p (k o) -> p k o", k=HT)
        wdv = wdall.rearrange("p (k o) -> p k o", k=HT)
        for k in range(HT):
            src = wgt_h[k * D:(k + 1) * D, :].rearrange("(c p) hh -> p c hh", p=128)
            dst = wgv[:, k, :].rearrange("p (c hh) -> p c hh", c=DT)
            S("G", lambda e, d=dst, s=src: e.dma_start(out=d, in_=s),
              writes=(f"wg{k}",), dma=True)
            src = wut_h[k * D:(k + 1) * D, :].rearrange("(c p) hh -> p c hh", p=128)
            dst = wuv[:, k, :].rearrange("p (c hh) -> p c hh", c=DT)
            S("G", lambda e, d=dst, s=src: e.dma_start(out=d, in_=s),
              writes=(f"wu{k}",), dma=True)
            src = wdt_h[k * 128:(k + 1) * 128, :]
            S("G", lambda e, d=wdv[:, k, :], s=src: e.dma_start(out=d, in_=s),
              writes=(f"wd{k}",), dma=True)

        x1tv = [x1t[i].rearrange("p (c t) -> p c t", c=DT) for i in range(2)]
        x1fvs = [x1fs[i].rearrange("p (c t) -> p c t", c=DT) for i in range(2)]

        def b_start(kt):
            buf = kt % 2
            srcp = x1h.rearrange("(c p) t -> p c t", p=128)[:, :, kt * FT:(kt + 1) * FT]
            S("S", lambda e, d=x1tv[buf], s=srcp: e.dma_start(out=d, in_=s),
              reads=(f"x1hbm{kt * FT // TT}",), writes=(f"x1t{buf}",), dma=True)
            S("A", lambda e, b=buf: e.copy(x1fs[b], x1t[b]),
              reads=(f"x1t{buf}",), writes=(f"x1f{buf}",))
            MM("p6", pb[6][:, 0:FT], onesrow, abrow[0:1, kt * FT:(kt + 1) * FT],
               True, True, ("abrow", "kconst"))
            MM("p6", pb[6][:, FT:2 * FT], onesrow, bbrow[0:1, kt * FT:(kt + 1) * FT],
               True, True, ("bbrow", "kconst"))
            S("A", lambda e: e.copy(bcab, pb[6][:, 0:2 * FT]),
              reads=("p6",), writes=("bcab",))
            for dd in range(DT):
                S("V", lambda e, dd=dd, b=buf: e.tensor_tensor(
                    out=x1tv[b][:, dd, :], in0=x1tv[b][:, dd, :],
                    in1=bcab[:, 0:FT], op=ALU.mult),
                    reads=("bcab", f"x1f{buf}"), writes=(f"xl2{buf}{dd}",))
                S("V", lambda e, dd=dd, b=buf: e.tensor_tensor(
                    out=x1tv[b][:, dd, :], in0=x1tv[b][:, dd, :],
                    in1=bcab[:, FT:2 * FT], op=ALU.add),
                    reads=("bcab",), writes=(f"xl2{buf}{dd}",))

        b_start(0)
        for kt in range(NBT):
            buf = kt % 2

            def down_mms(ht):
                for do in range(DT):
                    ob = do // 2
                    MM(f"p{ob}", pb[ob][:, (do % 2) * FT:(do % 2) * FT + FT],
                       wdv[:, ht, do * 128:(do + 1) * 128], hb[ht % 2],
                       ht == 0 and do % 2 == 0, ht == HT - 1,
                       (f"hb{ht % 2}", f"wd{ht}"))

            for ht in range(HT):
                bank = 4 + ht % 2
                for dd in range(DT):
                    MM(f"p{bank}", pb[bank][:, 0:FT],
                       wgv[:, ht, dd * 128:(dd + 1) * 128], x1tv[buf][:, dd, :],
                       dd == 0, dd == DT - 1, (f"xl2{buf}{dd}", f"wg{ht}"))
                for dd in range(DT):
                    MM(f"p{bank}", pb[bank][:, FT:2 * FT],
                       wuv[:, ht, dd * 128:(dd + 1) * 128], x1tv[buf][:, dd, :],
                       dd == 0, dd == DT - 1, (f"xl2{buf}{dd}", f"wu{ht}"))
                S("A", lambda e, b=bank, sb=silubs[ht % 2]: e.activation(
                    out=sb, in_=pb[b][:, 0:FT], func=AF.Silu, scale=SC),
                    reads=(f"p{bank}",), writes=(f"silub{ht % 2}",))
                S("V", lambda e, b=bank, w=ht % 2, sb=silubs[ht % 2]:
                    e.tensor_tensor(
                    out=hb[w], in0=pb[b][:, FT:2 * FT], in1=sb, op=ALU.mult),
                    reads=(f"p{bank}", f"silub{ht % 2}"), writes=(f"hb{ht % 2}",))
                if ht > 0:
                    down_mms(ht - 1)
                if ht == HT - 8 and kt + 1 < NBT:
                    b_start(kt + 1)
            down_mms(HT - 1)
            for do in range(DT):
                ob = do // 2
                S("V", lambda e, do=do, ob=ob, b=buf: e.scalar_tensor_tensor(
                    out=x1fvs[b][:, do, :],
                    in0=pb[ob][:, (do % 2) * FT:(do % 2) * FT + FT],
                    scalar=SC2, in1=x1fvs[b][:, do, :], op0=ALU.mult, op1=ALU.add),
                    reads=(f"p{ob}",), writes=(f"x1f{buf}",))
            dstp = outT.rearrange("(c p) t -> p c t", p=128)[:, :, kt * FT:(kt + 1) * FT]
            S("S", lambda e, d=dstp, s=x1fvs[buf]: e.dma_start(out=d, in_=s),
              reads=(f"x1f{buf}",), writes=(f"outhbm{kt}",), dma=True)

        with nc.Block() as block:
            def emit_stream(ename):
                def body(eng):
                    for (waits, fn, sem, amt) in sched.stream[ename]:
                        for (s, v) in waits:
                            eng.wait_ge(sems[s], v)
                        inst = fn(eng)
                        inst.then_inc(sems[sem], amt)
                    if ename == "S":
                        for s, v in sorted(sched.counts.items()):
                            eng.wait_ge(sems[s], v)
                return body

            block.tensor(emit_stream("T"))
            block.vector(emit_stream("V"))
            block.scalar(emit_stream("A"))
            block.gpsimd(emit_stream("G"))
            block.sync(emit_stream("S"))

    return nc


def _prep_inputs(x, wq, wk, wv, wo, wg1, wg2, norm_w, w_gate, w_up, w_down,
                 nown, npre, ncores, nseq):
    bf = ml_dtypes.bfloat16
    idx = np.arange(D, dtype=np.float64)
    theta = BASEF ** (-2.0 * idx / D)
    t = np.arange(nseq, dtype=np.float64)
    ang = np.outer(t, theta)
    cosv = np.cos(ang)
    sinv = np.sin(ang)
    cosT = cosv.T
    sinT = sinv.T

    ident = np.eye(128, dtype=bf)
    onescol = np.ones((128, 1), dtype=bf)
    onesrow = np.ones((1, 128), dtype=bf)
    tril = np.triu(np.ones((128, 128), dtype=np.float32))
    mask4 = np.ascontiguousarray(np.tile(tril, (1, 4))).astype(bf)
    blk2 = np.zeros((128, 2), dtype=np.float32)
    blk2[0:64, 0] = 1.0
    blk2[64:128, 1] = 1.0
    blk2t = np.ascontiguousarray(blk2.T).astype(bf)
    blk2 = blk2.astype(bf)
    onescan = np.ones((128, TT), dtype=bf)
    onesone = np.ones((1, 1), dtype=bf)
    normc = np.ascontiguousarray(
        np.asarray(norm_w, np.float32).reshape(DT, 128).T)

    wgT = np.ascontiguousarray(np.asarray(w_gate).T)
    wuT = np.ascontiguousarray(np.asarray(w_up).T)
    wgt = np.ascontiguousarray(
        wgT.reshape(D, HT, 128).transpose(1, 0, 2)).reshape(HT * D, 128)
    wut = np.ascontiguousarray(
        wuT.reshape(D, HT, 128).transpose(1, 0, 2)).reshape(HT * D, 128)
    wdt = np.ascontiguousarray(np.asarray(w_down).T)

    common = dict(
        wq=np.asarray(wq, np.float32), wk=np.asarray(wk, np.float32),
        wv=np.asarray(wv, np.float32), wo=np.asarray(wo, np.float32),
        wg1=np.asarray(wg1, np.float32), wg2=np.asarray(wg2, np.float32),
        normc=normc, ident=ident, onescol=onescol, onesrow=onesrow,
        mask4=mask4, blk2=blk2, blk2t=blk2t, onescan=onescan, onesone=onesone,
        wgt=wgt.astype(np.int32), wut=wut.astype(np.int32),
        wdt=wdt.astype(np.int32),
    )

    x = np.asarray(x, np.float32)
    in_maps = []
    for c in range(ncores):
        b = c // 2
        half = c % 2
        xtb = np.ascontiguousarray(x[b].T)
        off = half * nown
        m = dict(common)
        m["xo"] = np.ascontiguousarray(xtb[:, off:off + nown])
        m["xp"] = np.ascontiguousarray(xtb[:, 0:npre])
        m["cto"] = np.ascontiguousarray(cosT[:, off:off + nown]).astype(bf)
        m["sto"] = np.ascontiguousarray(sinT[:, off:off + nown]).astype(bf)
        m["ctp"] = np.ascontiguousarray(cosv[0:npre, :]).astype(bf)
        m["stp"] = np.ascontiguousarray(sinv[0:npre, :]).astype(bf)
        m["maskb"] = np.full((128, 1), float(half), dtype=np.float32)
        in_maps.append(m)
    return in_maps


def kernel(L0_attn_x, L0_attn_wq, L0_attn_wk, L0_attn_wv, L0_attn_wo,
           L0_attn_wg1, L0_attn_wg2, L0_attn_norm_w,
           L0_ffn_w_gate, L0_ffn_w_up, L0_ffn_w_down):
    x = np.asarray(L0_attn_x, np.float32)
    nb, nseq, d = x.shape
    assert d == D
    nown = nseq // 2
    npre = nown
    ncores = 2 * nb
    in_maps = _prep_inputs(x, L0_attn_wq, L0_attn_wk, L0_attn_wv, L0_attn_wo,
                           L0_attn_wg1, L0_attn_wg2, L0_attn_norm_w,
                           L0_ffn_w_gate, L0_ffn_w_up, L0_ffn_w_down,
                           nown, npre, ncores, nseq)
    nc = build_program(nown, npre)
    res = run_bass_kernel_spmd(nc, in_maps, core_ids=list(range(ncores)))
    out = np.empty((nb, nseq, D), dtype=np.float32)
    for c in range(ncores):
        b = c // 2
        half = c % 2
        out[b, half * nown:(half + 1) * nown, :] = res.results[c]["outT"].T
    return out
